# version 06
# baseline (speedup 1.0000x reference)
import numpy as np
import concourse.bass as bass
import concourse.mybir as mybir
from concourse.bass_utils import run_bass_kernel_spmd
from contextlib import ExitStack

F32 = mybir.dt.float32
BF16 = mybir.dt.bfloat16
FP16 = mybir.dt.float16
F8E5 = mybir.dt.float8e5
I32 = mybir.dt.int32
ALU = mybir.AluOpType
ACT = mybir.ActivationFunctionType
AX = mybir.AxisListType

B = 2
T = 8192
DM = 1024
DI = 2048
NS = 16
DSL = 512
NDB = DSL // 128
TQ = 2048
TC = 1024
NCH = T // TC
FC = 512
NFC = TQ // FC
LN_EPS = 1e-5
GROUPS = [[0, 1, 2, 3], [4, 5, 6, 7]]
DEBUG = False


class Sched:
    ENGINES = ("sync", "gpsimd", "scalar", "vector", "tensor")

    def __init__(self):
        self.ops = {e: [] for e in self.ENGINES}
        self.count = {}
        self.waited = {e: {} for e in self.ENGINES}

    def emit(self, eng, fn):
        self.ops[eng].append(fn)

    def produce(self, eng, sem, fn, inc):
        self.count[sem] = self.count.get(sem, 0) + inc
        val = self.count[sem]

        def op(e, fn=fn, sem=sem, inc=inc):
            fn(e).then_inc(e._sems[sem], inc)
        self.emit(eng, op)
        return (sem, val)

    def wait(self, eng, *tokens):
        for tok in tokens:
            if tok is None:
                continue
            sem, val = tok
            if self.waited[eng].get(sem, -1) >= val:
                continue
            self.waited[eng][sem] = val

            def op(e, sem=sem, val=val):
                e.wait_ge(e._sems[sem], val)
            self.emit(eng, op)

    def mark(self, sem):
        return (sem, self.count.get(sem, 0))


def build_nc():
    nc = bass.Bass("TRN2", target_bir_lowering=False, debug=False)
    S = Sched()

    def ein(name, shape, dt=F32):
        return nc.dram_tensor(name, list(shape), dt, kind="ExternalInput").ap()

    x_own = ein("x_own", (TQ, DM))
    winT = ein("winT", (DM, 2 * DSL))
    convd = ein("convd", (NDB * 4, 128, 128))
    convb_i = ein("convb", (128, NDB))
    xpwT = ein("xpwT", (DSL, 33))
    dtw_i = ein("dtw", (128, NDB))
    dtb_i = ein("dtb", (128, NDB))
    logA_i = ein("logA", (NDB, 128, NS))
    Dprm_i = ein("Dprm", (128, NDB))
    nw_i = ein("nw", (128, NDB))
    nb_i = ein("nb", (128, NDB))
    woutT = ein("woutT", (DSL, DM))
    w1T = ein("w1T", (DM, 4 * DM))
    b1_i = ein("b1", (128, 32))
    w2T = ein("w2T", (4 * DM, DM))
    b2_i = ein("b2", (128, 8))
    maskT = ein("maskT", (DM, TQ), FP16)
    ident_i = ein("ident", (128, 128))

    out_d = nc.dram_tensor("out", [TQ, DM], F32, kind="ExternalOutput").ap()

    xn_own_d = nc.dram_tensor("xn_own_d", [DM, TQ], F32).ap()
    xn_all_d = nc.dram_tensor("xn_all_d", [4 * DM, TQ], F32).ap()
    w18_d = nc.dram_tensor("w18_d", [8, 128, 4 * DM], F8E5).ap()
    w28_d = nc.dram_tensor("w28_d", [32, 128, DM], F8E5).ap()
    ssm_in_d = nc.dram_tensor("ssm_in_d", [NCH, 33, TC], F32).ap()
    ssm_out_d = nc.dram_tensor("ssm_out_d", [NCH, 33, TC], F32).ap()
    ssmh_d = nc.dram_tensor("ssmh_d", [33, TC], FP16).ap()
    mrs_d = nc.dram_tensor("mrs_d", [2, TC], FP16).ap()
    st_in_d = nc.dram_tensor("st_in_d", [NCH, 2, TC], F32).ap()
    st_out_d = nc.dram_tensor("st_out_d", [NCH, 2, TC], F32).ap()
    sp_in_d = nc.dram_tensor("sp_in_d", [4, DM, TQ], FP16).ap()
    s_own_d = nc.dram_tensor("s_own_d", [DM, TQ], FP16).ap()
    g_own_d = nc.dram_tensor("g_own_d", [DM, TQ], FP16).ap()

    stk = ExitStack()

    def sbuf(name, shape, dt):
        return stk.enter_context(nc.sbuf_tensor(name, list(shape), dt)).ap()

    def psum(name, shape, dt=F32):
        return stk.enter_context(nc.psum_tensor(name, list(shape), dt)).ap()

    win_sb = sbuf("win_sb", (128, 8, 2 * DSL), BF16)
    wout_sb = sbuf("wout_sb", (128, NDB, DM), BF16)
    convd_sb = sbuf("convd_sb", (128, NDB * 4, 128), BF16)
    xpw_sb = sbuf("xpw_sb", (128, NDB, 33), BF16)
    w1sl_sb = sbuf("w1sl_sb", (128, 2, TC), F8E5)
    w2sl_sb = sbuf("w2sl_sb", (128, 2, TC), F8E5)
    cols_sb = sbuf("cols_sb", (128, 6, NDB), F32)
    b1q_sb = sbuf("b1q_sb", (128, 32), F32)
    b2q_sb = sbuf("b2q_sb", (128, 8), F32)
    b18_sb = sbuf("b18_sb", (128, 32), F8E5)
    b28_sb = sbuf("b28_sb", (128, 8), F8E5)
    negA_sb = sbuf("negA_sb", (128, NDB * NS), F32)
    ones_sb = sbuf("ones_sb", (128, 1), BF16)
    ident_sb = sbuf("ident_sb", (128, 128), F32)

    xnb_sb = sbuf("xnb_sb", (128, 8, TC), BF16)
    xnr_sb = sbuf("xnr_sb", (128, 2, TC), F32)
    xpr_sb = sbuf("xpr_sb", (128, NDB, TC + 4), BF16)
    xcr_sb = sbuf("xcr_sb", (128, 2, NDB, TC), BF16)
    zsil_sb = sbuf("zsil_sb", (128, 2, NDB, TC), FP16)
    ssm_sb = sbuf("ssm_sb", (128, 2, TC), F32)
    ssmh_sb = sbuf("ssmh_sb", (128, TC), FP16)

    arenaA = sbuf("arenaA", (128, 16384), F8E5)
    arenaB = sbuf("arenaB", (128, 4096), F32)
    drawbc_sb = sbuf("drawbc_sb", (128, TC), F32)
    dtmp_sb = sbuf("dtmp_sb", (128, TC), F32)
    du_sb = sbuf("du_sb", (128, NDB, TC), FP16)
    dA_sb = sbuf("dA_sb", (128, 2, TC), F32)
    q_sb = sbuf("q_sb", (128, 2, TC), FP16)
    bbc_sb = sbuf("bbc_sb", (128, 2, TC), FP16)
    cbc_sb = sbuf("cbc_sb", (128, 2, TC), FP16)
    mubc_sb = sbuf("mubc_sb", (128, TC), FP16)
    rsbc_sb = sbuf("rsbc_sb", (128, TC), FP16)
    state_sb = sbuf("state_sb", (128, NDB * NS), F32)
    y_sb = sbuf("y_sb", (128, NDB, TC), F32)
    gated_sb = sbuf("gated_sb", (128, NDB, TC), BF16)
    sev_sb = sbuf("sev_sb", (128, 2, 512), FP16)
    lncol_sb = sbuf("lncol_sb", (128, 16), F32)
    xnp_sb = sbuf("xnp_sb", (128, 2, 128), F32)
    dbgc_sb = sbuf("dbgc_sb", (128, 16, 4), F32) if DEBUG else None

    xt_sb = xcr_sb.bitcast(F32).rearrange("p a b d -> p (a b d)")[:, 0:2 * DM]\
        .rearrange("p (a d) -> p a d", d=DM)
    h8_v = arenaA.rearrange("p (a d) -> p a d", d=FC)
    G_v = arenaA.bitcast(BF16)[:, 0:4 * TC].rearrange("p (a d) -> p a d", d=TC)
    L1_v = arenaA.bitcast(FP16)[:, 4 * TC:6 * TC].rearrange("p (a d) -> p a d", d=TC)
    l2f_v = arenaA.bitcast(FP16)[:, 6 * TC:7 * TC]
    delta_v = arenaB.rearrange("p (a d) -> p a d", d=TC)
    xn8_v = arenaB.bitcast(F8E5)[:, 0:4096].rearrange("p (a d) -> p a d", d=FC)
    xnf_v = arenaB[:, 1024:2048].rearrange("p (a d) -> p a d", d=FC)
    hbf_v = arenaB.bitcast(BF16)[:, 4096:5120].rearrange("p (a d) -> p a d", d=FC)
    y2b_v = arenaB.bitcast(BF16)[:, 5120:6144].rearrange("p (a d) -> p a d", d=FC)
    yd_v = arenaB[:, 3072:4096].rearrange("p (a d) -> p a d", d=FC)
    g16_v = dtmp_sb.bitcast(FP16)[:, 0:1024].rearrange("p (a d) -> p a d", d=FC)
    msk_v = dtmp_sb.bitcast(FP16)[:, 1024:2048].rearrange("p (a d) -> p a d", d=FC)
    s3_v = arenaB.bitcast(FP16)[:, 0:2 * DM].rearrange("p (a d) -> p a d", d=DM)
    g3_v = arenaB.bitcast(FP16)[:, 2 * DM:4 * DM].rearrange("p (a d) -> p a d", d=DM)
    o3_v = arenaB[:, 2048:4096].rearrange("p (a d) -> p a d", d=DM)
    ybf_v = ssmh_sb.bitcast(BF16)
    ysq_v = q_sb.bitcast(BF16)[:, 0, :]
    r_sy = dA_sb[0:1, 0, :]
    r_sy2 = dA_sb[0:1, 1, :]
    r_tmp = drawbc_sb[0:1, :]
    r_h16 = q_sb[0:1, 1, :]

    pb = [psum(f"pb{i}", (128, 512)) for i in range(8)]

    sems = {}
    for s_ in ("sd", "gd", "cc", "mm", "vv", "ss", "si", "si2", "w8w",
               "xl0", "xl1", "xw0", "xw1", "xf0", "xf1", "w1a", "w1b",
               "w2a", "w2b", "mk0", "mk1", "gw0", "gw1", "xr0", "xr1",
               "ssw", "sr0", "sr1", "shw", "dbc", "bb0", "bb1", "cb0",
               "cb1", "stw", "sur", "mw1", "mw2", "mub", "rsb", "sp0",
               "sp1", "x30", "x31", "s30", "s31", "g30", "g31", "ow", "dbg"):
        sems[s_] = stk.enter_context(nc.semaphore(s_))

    def dma(eng, out, in_, sem="sd", **kw):
        return S.produce(eng, sem,
                         lambda e, o=out, i=in_, kw=kw: e.dma_start(out=o, in_=i, **kw), 16)

    def tdma(eng, out, in_, sem="sd"):
        return S.produce(eng, sem,
                         lambda e, o=out, i=in_: e.dma_start_transpose(o, i), 16)

    def mm(lhsT, rhs_, out, start, stop):
        return S.produce("tensor", "mm",
                         lambda e, a=lhsT, b_=rhs_, o=out, st=start, sp=stop:
                         e.matmul(o, a, b_, start=st, stop=sp), 1)

    def act(out, in_, func, bias=0.0, scale=1.0):
        return S.produce("scalar", "ss",
                         lambda e, o=out, i=in_, f=func, b_=bias, sc=scale:
                         e.activation(out=o, in_=i, func=f, bias=b_, scale=sc), 1)

    def vop(fn):
        return S.produce("vector", "vv", fn, 1)

    def vdrain():
        S.emit("vector", lambda e: e.drain())

    def coll(ins, outs, op=ALU.add, kind="AllReduce"):
        tok = S.produce("gpsimd", "cc",
                        lambda e, i=ins, o=outs, k=kind, p=op:
                        e.collective_compute(k, p, replica_groups=GROUPS,
                                             ins=[i.opt()], outs=[o.opt()]), 1)
        S.wait("gpsimd", tok)
        return tok

    xnb_flat = xnb_sb.rearrange("p a d -> p (a d)")
    stage_v = xnb_flat.bitcast(F32)[:, 0:4096]
    stage8_v = xnb_flat.bitcast(F8E5)
    wtoks = []
    prev_cast = None

    def staged(src, cols, dst):
        nonlocal prev_cast
        if prev_cast is not None:
            S.wait("sync", prev_cast)
        t_l = dma("sync", stage_v[:, 0:cols], src, sem="si")
        S.wait("vector", t_l)
        prev_cast = vop(lambda e, o=dst, i=stage_v[:, 0:cols]: e.tensor_copy(o, i))
        wtoks.append(prev_cast)
        return prev_cast

    for kb in range(8):
        staged(winT[kb * 128:(kb + 1) * 128, :], 2 * DSL, win_sb[:, kb, :])
    for kb in range(NDB):
        staged(woutT[kb * 128:(kb + 1) * 128, :], DM, wout_sb[:, kb, :])
    for pl in range(NDB * 4):
        staged(convd[pl], 128, convd_sb[:, pl, :])
    for kb in range(NDB):
        staged(xpwT[kb * 128:(kb + 1) * 128, :], 33, xpw_sb[:, kb, :])
    w8_dmas = []
    for kb in range(8):
        for hf in range(2):
            if prev_cast is not None:
                S.wait("sync", prev_cast)
            t_l = dma("sync", stage_v[:, 0:2048],
                      w1T[kb * 128:(kb + 1) * 128, hf * 2048:(hf + 1) * 2048], sem="si")
            S.wait("vector", t_l)
            t_c = vop(lambda e, o=stage8_v[:, 8192:10240], i=stage_v[:, 0:2048]:
                      e.tensor_copy(o, i))
            S.wait("gpsimd", t_c)
            t_d = dma("gpsimd", w18_d[kb][:, hf * 2048:(hf + 1) * 2048],
                      stage8_v[:, 8192:10240], sem="w8w")
            w8_dmas.append(t_d)
            prev_cast = t_d
            wtoks.append(t_c)
    for kb in range(32):
        if prev_cast is not None:
            S.wait("sync", prev_cast)
        t_l = dma("sync", stage_v[:, 0:1024], w2T[kb * 128:(kb + 1) * 128, :], sem="si")
        S.wait("vector", t_l)
        t_c = vop(lambda e, o=stage8_v[:, 8192:9216], i=stage_v[:, 0:1024]:
                  e.tensor_copy(o, i))
        S.wait("gpsimd", t_c)
        t_d = dma("gpsimd", w28_d[kb], stage8_v[:, 8192:9216], sem="w8w")
        w8_dmas.append(t_d)
        prev_cast = t_d
        wtoks.append(t_c)

    t_cb = dma("sync", cols_sb[:, 0, :], convb_i, sem="si2")
    t_dw = dma("sync", cols_sb[:, 1, :], dtw_i, sem="si2")
    t_db = dma("sync", cols_sb[:, 2, :], dtb_i, sem="si2")
    t_Dp = dma("sync", cols_sb[:, 3, :], Dprm_i, sem="si2")
    t_nw = dma("sync", cols_sb[:, 4, :], nw_i, sem="si2")
    t_nb = dma("sync", cols_sb[:, 5, :], nb_i, sem="si2")
    t_b1 = dma("sync", b1q_sb, b1_i, sem="si2")
    t_b2 = dma("sync", b2q_sb, b2_i, sem="si2")
    t_id = dma("sync", ident_sb, ident_i, sem="si2")
    la_toks = [dma("sync", negA_sb.rearrange("p (a b) -> p a b", b=NS)[:, db, :], logA_i[db], sem="si2")
               for db in range(NDB)]
    si2_all = S.mark("si2")
    cols_toks = [si2_all]
    S.wait("vector", si2_all)
    tb = vop(lambda e: e.tensor_copy(b18_sb, b1q_sb))
    S.wait("vector", tb)
    wtoks.append(vop(lambda e: e.tensor_copy(b1q_sb, b18_sb)))
    tb = vop(lambda e: e.tensor_copy(b28_sb, b2q_sb))
    S.wait("vector", tb)
    wtoks.append(vop(lambda e: e.tensor_copy(b2q_sb, b28_sb)))
    S.wait("scalar", si2_all)
    tA = act(dtmp_sb[:, 0:NDB * NS], negA_sb, ACT.Exp)
    S.wait("vector", tA)
    vdrain()
    wtoks.append(vop(lambda e: e.tensor_scalar(negA_sb, dtmp_sb[:, 0:NDB * NS],
                                               -1.0, None, ALU.mult)))
    vdrain()
    wtoks.append(vop(lambda e: e.memset(ones_sb, 1.0)))
    wtoks.append(vop(lambda e: e.memset(state_sb, 0.0)))
    wtoks.append(vop(lambda e: e.memset(xpr_sb[:, :, 0:4], 0.0)))

    S.wait("tensor", si2_all)
    xn_dmas = []
    bank_free = [None, None]
    par_free = [None, None]
    xnp_free = [None, None]
    s9_prev = [None, None]
    prev_v2 = None
    for tt in range(TQ // 128):
        par = tt % 2
        xtile = xt_sb[:, par, :]
        lnc = lncol_sb[:, par * 8: par * 8 + 8: 2]
        S.wait("sync", par_free[par])
        t_l = dma("sync", xtile, x_own[tt * 128:(tt + 1) * 128, :], sem="xl0" if par == 0 else "xl1")
        S.wait("vector", t_l, s9_prev[par])
        vdrain()
        vop(lambda e, o=lnc[:, 0:1], i=xtile: e.tensor_reduce(o, i, AX.X, ALU.add))
        vdrain()
        vop(lambda e, o=dtmp_sb[:, 0:DM], i=xtile: e.tensor_tensor(o, i, i, ALU.mult))
        vdrain()
        prev_v2 = vop(lambda e, o=lnc[:, 1:2], i=dtmp_sb[:, 0:DM]:
                      e.tensor_reduce(o, i, AX.X, ALU.add))
        vdrain()
        if DEBUG:
            vop(lambda e, o=dbgc_sb[:, tt, 2:4], i=lnc[:, 0:2]: e.tensor_copy(o, i))
        vop(lambda e, c_=lnc: e.tensor_scalar(c_[:, 0:1], c_[:, 0:1],
                                              1.0 / DM, None, ALU.mult))
        vdrain()
        vop(lambda e, c_=lnc: e.tensor_scalar(c_[:, 1:2], c_[:, 1:2],
                                              1.0 / DM, None, ALU.mult))
        vdrain()
        vop(lambda e, c_=lnc: e.tensor_scalar(c_[:, 2:3], c_[:, 0:1],
                                              c_[:, 0:1], None, ALU.mult))
        vdrain()
        vop(lambda e, c_=lnc: e.tensor_tensor(c_[:, 1:2], c_[:, 1:2],
                                              c_[:, 2:3], ALU.subtract))
        vdrain()
        vop(lambda e, c_=lnc: e.tensor_scalar(c_[:, 1:2], c_[:, 1:2],
                                              LN_EPS, None, ALU.add))
        vdrain()
        v8 = vop(lambda e, c_=lnc: e.reciprocal(c_[:, 2:3], c_[:, 1:2]))
        vdrain()
        S.wait("scalar", v8)
        s9 = act(lnc[:, 3:4], lnc[:, 2:3], ACT.Sqrt)
        s9_prev[par] = s9
        S.wait("vector", s9)
        vop(lambda e, c_=lnc: e.tensor_scalar(c_[:, 2:3], c_[:, 3:4],
                                              c_[:, 3:4], None, ALU.mult))
        vdrain()
        vop(lambda e, c_=lnc: e.tensor_tensor(c_[:, 2:3], c_[:, 2:3],
                                              c_[:, 1:2], ALU.mult))
        vdrain()
        vop(lambda e, c_=lnc: e.tensor_scalar(c_[:, 2:3], c_[:, 2:3],
                                              -0.5, 1.5, ALU.mult, ALU.add))
        vdrain()
        vop(lambda e, c_=lnc: e.tensor_tensor(c_[:, 3:4], c_[:, 3:4],
                                              c_[:, 2:3], ALU.mult))
        vdrain()
        if DEBUG:
            vop(lambda e, o=dbgc_sb[:, tt, 0:2], i=lnc[:, 0:2]: e.tensor_copy(o, i))
        v10 = vop(lambda e, o=xtile, i=xtile, c_=lnc: e.tensor_scalar(
            o, i, c_[:, 0:1], c_[:, 3:4], ALU.subtract, ALU.mult))
        last_tp = None
        for dbm in range(8):
            bk = dbm % 2
            bank = pb[bk]
            S.wait("tensor", v10, bank_free[bk])
            t_tp = S.produce("tensor", "mm",
                             lambda e, o=bank[:, 0:128],
                             i=xtile[:, dbm * 128:(dbm + 1) * 128],
                             idn=ident_sb: e.transpose(o, i, idn), 1)
            last_tp = t_tp
            S.wait("vector", t_tp, xnp_free[bk])
            t_ev = vop(lambda e, o=xnp_sb[:, bk, :], i=bank[:, 0:128]: e.tensor_copy(o, i))
            bank_free[bk] = t_ev
            S.wait("sync", t_ev)
            last_d = dma("sync", xn_own_d[dbm * 128:(dbm + 1) * 128, tt * 128:(tt + 1) * 128],
                         xnp_sb[:, bk, :], sem="xw0" if bk == 0 else "xw1")
            xn_dmas.append(last_d)
            xnp_free[bk] = last_d
        par_free[par] = last_tp
    S.wait("gpsimd", *xn_dmas)
    t_ag = coll(xn_own_d, xn_all_d, op=ALU.bypass, kind="AllGather")
    pb_p0 = list(bank_free)

    S.wait("vector", *wtoks)
    S.wait("sync", *w8_dmas)
    S.wait("sync", *xn_dmas)
    g_dmas = []
    w1_slab_free = [None, None]
    w2_slab_free = [None, None]
    pb_free = [None] * 8
    pb_free[0], pb_free[1] = pb_p0[0], pb_p0[1]
    xnf_war = [None, None]
    hbf_war = [None, None]
    y2b_war = [None, None]
    yd_war = [None, None]
    g16_war = [None, None]
    msk_war = [None, None]
    prev_fc_mm = None
    for fc in range(NFC):
        S.wait("vector", prev_fc_mm)
        x8_toks = []
        for kb in range(8):
            rp = kb % 2
            S.wait("sync", xnf_war[rp])
            t_l = dma("sync", xnf_v[:, rp, :],
                      xn_own_d[kb * 128:(kb + 1) * 128, fc * FC:(fc + 1) * FC],
                      sem="xf0" if rp == 0 else "xf1")
            S.wait("vector", t_l)
            ct = vop(lambda e, o=xn8_v[:, kb, :], i=xnf_v[:, rp, :]: e.tensor_copy(o, i))
            x8_toks.append(ct)
            xnf_war[rp] = ct
        h8_toks = []
        for mg in range(4):
            mm_last = [None] * 8
            for kb in range(8):
                sp_ = kb % 2
                S.wait("sync", w1_slab_free[sp_])
                t_w = dma("sync", w1sl_sb[:, sp_, :],
                          w18_d[kb][:, mg * TC:(mg + 1) * TC],
                          sem="w1a" if sp_ == 0 else "w1b")
                if DEBUG and fc == 0 and mg == 0 and kb < 2:
                    if not hasattr(build_nc, "_dbg_sl"):
                        build_nc._dbg_sl = nc.dram_tensor(
                            "dbg_sl", [2, 128, TC], F8E5, kind="ExternalOutput").ap()
                    S.wait("gpsimd", t_w)
                    t_dsl = dma("gpsimd", build_nc._dbg_sl[kb], w1sl_sb[:, sp_, :], sem="dbg")
                    S.wait("tensor", t_dsl)
                S.wait("tensor", t_w, *x8_toks)
                if kb == 0:
                    S.wait("tensor", *pb_free)
                for mb8 in range(8):
                    mm_last[mb8] = mm(w1sl_sb[:, sp_, mb8 * 128:(mb8 + 1) * 128],
                                      xn8_v[:, kb, :], pb[mb8],
                                      start=(kb == 0), stop=(kb == 7))
                w1_slab_free[sp_] = mm_last[7]
            for mb8 in range(8):
                mb = mg * 8 + mb8
                S.wait("scalar", mm_last[mb8], hbf_war[mb8 % 2])
                t_h = act(hbf_v[:, mb8 % 2, :], pb[mb8], ACT.Relu,
                          bias=b1q_sb[:, mb:mb + 1])
                S.wait("vector", t_h)
                t_8 = vop(lambda e, o=h8_v[:, mb, :], i=hbf_v[:, mb8 % 2, :]:
                          e.tensor_copy(o, i))
                h8_toks.append(t_8)
                hbf_war[mb8 % 2] = t_8
                pb_free[mb8] = t_h
        if DEBUG and fc == 0:
            dbg_w18 = nc.dram_tensor("dbg_w18", [8, 128, 4 * DM], F8E5, kind="ExternalOutput").ap()
            dma("gpsimd", dbg_w18[:, :, :], w18_d[:, :, :], sem="dbg")
            dbg_x8 = nc.dram_tensor("dbg_x8", [128, 4096], F8E5, kind="ExternalOutput").ap()
            dbg_h8 = nc.dram_tensor("dbg_h8", [128, 16384], F8E5, kind="ExternalOutput").ap()
            S.wait("gpsimd", *h8_toks)
            S.wait("gpsimd", *x8_toks)
            dma("gpsimd", dbg_x8[:, :], arenaB.bitcast(F8E5)[:, 0:4096], sem="dbg")
            dma("gpsimd", dbg_h8[:, :], arenaA[:, :], sem="dbg")
        mm_last = [None] * 8
        for kb in range(32):
            sp_ = kb % 2
            S.wait("sync", w2_slab_free[sp_])
            t_w = dma("sync", w2sl_sb[:, sp_, :], w28_d[kb], sem="w2a" if sp_ == 0 else "w2b")
            S.wait("tensor", t_w, *h8_toks)
            if kb == 0:
                S.wait("tensor", *pb_free)
            for mb in range(8):
                mm_last[mb] = mm(w2sl_sb[:, sp_, mb * 128:(mb + 1) * 128],
                                 h8_v[:, kb, :], pb[mb],
                                 start=(kb == 0), stop=(kb == 31))
            w2_slab_free[sp_] = mm_last[7]
        for mb in range(8):
            sl = mb % 2
            S.wait("scalar", mm_last[mb], y2b_war[sl])
            t_y2 = act(y2b_v[:, sl, :], pb[mb], ACT.Identity,
                       bias=b2q_sb[:, mb:mb + 1])
            S.wait("sync", msk_war[sl])
            t_m = dma("sync", msk_v[:, sl, :],
                      maskT[mb * 128:(mb + 1) * 128, fc * FC:(fc + 1) * FC],
                      sem="mk0" if sl == 0 else "mk1")
            S.wait("vector", t_y2, t_m, yd_war[sl])
            t_yd = vop(lambda e, o=yd_v[:, sl, :], a=y2b_v[:, sl, :],
                       b_=msk_v[:, sl, :]: e.tensor_tensor(o, a, b_, ALU.mult))
            y2b_war[sl] = t_yd
            msk_war[sl] = t_yd
            S.wait("scalar", t_yd, g16_war[sl])
            t_g = act(g16_v[:, sl, :], yd_v[:, sl, :], ACT.Silu)
            yd_war[sl] = t_g
            S.wait("gpsimd", t_g)
            t_gd = dma("gpsimd", g_own_d[mb * 128:(mb + 1) * 128, fc * FC:(fc + 1) * FC],
                       g16_v[:, sl, :], sem="gw0" if sl == 0 else "gw1")
            g_dmas.append(t_gd)
            g16_war[sl] = t_gd
            pb_free[mb] = t_y2
        prev_fc_mm = S.mark("mm")
    p1_v = S.mark("vv")
    p1_s = S.mark("ss")
    p1_g = g_dmas[-1]
    p1_mm = S.mark("mm")

    S.wait("sync", t_ag)
    S.wait("scalar", *cols_toks)

    xnb_war = None
    xnr_war = [None, None]
    xpr_mm_war = None
    xpr_tail_tok = None
    xcr_war = [None, None]
    zsil_war = [None, None]
    ssm_war = [None, None]
    ip_evac = [None, None]
    cv_evac = [None, None]
    xq_evac = [None, None]
    st_evac = [None, None, None, None]
    op_evac = [None, None]
    sev_war = [None, None]
    prev_scan_v = None
    bbc_war = [None, None]
    cbc_war = [None, None]
    dAq_war = [None, None]
    mrs_war = None
    mrs_read_gate = [None]
    dexp_war = None
    norm_war = None
    arena_gate_done = False

    for c in range(NCH):
        par = c % 2
        qblk = (c * TC) // TQ
        tcol = (c * TC) % TQ
        cast_toks = []
        for kb in range(8):
            rp = kb % 2
            S.wait("sync", xnr_war[rp])
            t_l = dma("sync", xnr_sb[:, rp, :],
                      xn_all_d[qblk * DM + kb * 128: qblk * DM + (kb + 1) * 128,
                               tcol:tcol + TC], sem="xr0" if rp == 0 else "xr1")
            S.wait("vector", t_l, xnb_war)
            ct = vop(lambda e, o=xnb_sb[:, kb, :], i=xnr_sb[:, rp, :]: e.tensor_copy(o, i))
            cast_toks.append(ct)
            xnr_war[rp] = ct
        t_ip_last = None
        for mb in range(8):
            for th in range(2):
                bk = (mb * 2 + th) % 2
                bank = pb[bk]
                S.wait("tensor", *cast_toks)
                S.wait("tensor", ip_evac[bk], p1_s if c == 0 else None)
                toks = [mm(win_sb[:, kb, mb * 128:(mb + 1) * 128],
                           xnb_sb[:, kb, th * 512:(th + 1) * 512],
                           bank, start=(kb == 0), stop=(kb == 7)) for kb in range(8)]
                t_ip_last = toks[-1]
                if mb < 4:
                    S.wait("vector", toks[-1], xpr_mm_war)
                    ev = vop(lambda e, o=xpr_sb[:, mb, 4 + th * 512: 4 + (th + 1) * 512],
                             i=bank: e.tensor_copy(o, i))
                else:
                    S.wait("scalar", toks[-1], zsil_war[par])
                    ev = act(zsil_sb[:, par, mb - 4, th * 512:(th + 1) * 512],
                             bank, ACT.Silu)
                ip_evac[bk] = ev
        xnb_war = t_ip_last
        xp_done_v = S.mark("vv")
        for db in range(NDB):
            for th in range(2):
                bk = 2 + th
                bank = pb[bk]
                S.wait("tensor", ip_evac[0], ip_evac[1], xp_done_v)
                S.wait("tensor", cv_evac[th], xpr_tail_tok)
                toks = []
                for tap in range(4):
                    mv = xpr_sb[:, db, 1 + tap + th * 512: 513 + tap + th * 512]
                    toks.append(mm(convd_sb[:, db * 4 + tap, :], mv, bank,
                                   start=(tap == 0), stop=(tap == 3)))
                S.wait("scalar", toks[-1], xcr_war[par])
                ev = act(xcr_sb[:, par, db, th * 512:(th + 1) * 512], bank,
                         ACT.Silu, bias=cols_sb[:, 0, db:db + 1])
                cv_evac[th] = ev
                xpr_mm_war = toks[-1]
        conv_all_s = S.mark("ss")
        S.wait("vector", xpr_mm_war)
        xpr_tail_tok = vop(lambda e: e.tensor_copy(xpr_sb[:, :, 0:4],
                                                   xpr_sb[:, :, TC:TC + 4]))
        for th in range(2):
            bank = pb[4 + th]
            S.wait("tensor", conv_all_s)
            S.wait("tensor", xq_evac[th], st_evac[th])
            toks = [mm(xpw_sb[:, db, :], xcr_sb[:, par, db, th * 512:(th + 1) * 512],
                       bank[0:33, :], start=(db == 0), stop=(db == NDB - 1))
                    for db in range(NDB)]
            S.wait("vector", toks[-1], ssm_war[par])
            ev = vop(lambda e, o=ssm_sb[0:33, par, th * 512:(th + 1) * 512],
                     i=bank[0:33, :]: e.tensor_copy(o, i))
            xq_evac[th] = ev
        S.wait("gpsimd", xq_evac[0], xq_evac[1])
        t_sd = dma("gpsimd", ssm_in_d[c], ssm_sb[0:33, par, :], sem="ssw")
        S.wait("gpsimd", t_sd)
        t_ar = coll(ssm_in_d[c], ssm_out_d[c])

        if c == 0:
            S.wait("vector", p1_v, p1_g, p1_mm, p1_s)
            S.wait("scalar", p1_v, p1_g, p1_mm)
        S.wait("sync", t_ar)
        t_rb = dma("sync", ssm_sb[0:33, par, :], ssm_out_d[c], sem="sr0" if par == 0 else "sr1")
        S.wait("vector", t_rb)
        t_h16 = vop(lambda e, o=ssmh_sb[0:33, :], i=ssm_sb[0:33, par, :]: e.tensor_copy(o, i))
        ssm_war[par] = t_h16
        S.wait("gpsimd", t_h16, prev_scan_v)
        if DEBUG and c == 0:
            dbg_shs = nc.dram_tensor("dbg_shs", [33, TC], FP16, kind="ExternalOutput").ap()
            dma("gpsimd", dbg_shs[:, :], ssmh_sb[0:33, :], sem="dbg")
        t_hd = dma("gpsimd", ssmh_d, ssmh_sb[0:33, :], sem="shw")
        if DEBUG and c == 0:
            dbg_sh = nc.dram_tensor("dbg_sh", [33, TC], FP16, kind="ExternalOutput").ap()
            dbg_sbp = nc.dram_tensor("dbg_sbp", [33, TC], F32, kind="ExternalOutput").ap()
            S.wait("gpsimd", t_hd)
            dma("gpsimd", dbg_sh[:, :], ssmh_d[:, :], sem="dbg")
            dma("gpsimd", dbg_sbp[:, :], ssm_sb[0:33, par, :], sem="dbg")
            dtok2 = S.mark("dbg")
            S.wait("sync", dtok2)
        S.wait("sync", dexp_war, norm_war, t_ar)
        t_dbc = dma("sync", drawbc_sb, ssm_out_d[c, 32:33, :].partition_broadcast(128), sem="dbc")
        for db in range(NDB):
            S.wait("scalar", t_dbc)
            t_e = act(dtmp_sb, drawbc_sb, ACT.Exp,
                      scale=cols_sb[:, 1, db:db + 1], bias=cols_sb[:, 2, db:db + 1])
            S.wait("scalar", t_e)
            t_ln = act(delta_v[:, db, :], dtmp_sb, ACT.Ln, bias=1.0)
            dexp_war = t_ln
            S.wait("vector", t_ln, conv_all_s)
            vop(lambda e, o=delta_v[:, db, :]: e.tensor_scalar(
                o, o, 0.001, 0.1, ALU.max, ALU.min))
            t_du = vop(lambda e, o=du_sb[:, db, :], a=delta_v[:, db, :],
                       b_=xcr_sb[:, par, db, :]: e.tensor_tensor(o, a, b_, ALU.mult))
        delta_all_v = S.mark("vv")
        for db in range(NDB):
            t_y0 = vop(lambda e, o=y_sb[:, db, :], i=xcr_sb[:, par, db, :],
                       d_=cols_sb[:, 3, db:db + 1]:
                       e.tensor_scalar(o, i, d_, None, ALU.mult))
            xcr_war[par] = t_y0
            for nq in range(4):
                for nn in range(4):
                    n = nq * 4 + nn
                    rot = n % 2
                    ap2 = n % 2
                    S.wait("sync", t_hd, bbc_war[rot], cbc_war[rot])
                    t_bb = dma("sync", bbc_sb[:, rot, :],
                               ssmh_d[n:n + 1, :].partition_broadcast(128),
                               sem="bb0" if rot == 0 else "bb1")
                    t_cb2 = dma("sync", cbc_sb[:, rot, :],
                                ssmh_d[16 + n:17 + n, :].partition_broadcast(128),
                                sem="cb0" if rot == 0 else "cb1")
                    S.wait("scalar", delta_all_v, dAq_war[ap2])
                    t_dA = act(dA_sb[:, ap2, :], delta_v[:, db, :], ACT.Exp,
                               scale=negA_sb[:, db * NS + n:db * NS + n + 1])
                    S.wait("vector", t_bb)
                    t_q = vop(lambda e, o=q_sb[:, ap2, :], a=du_sb[:, db, :],
                              b_=bbc_sb[:, rot, :]: e.tensor_tensor(o, a, b_, ALU.mult))
                    bbc_war[rot] = t_q
                    S.wait("vector", t_dA, t_cb2)
                    vdrain()
                    t_sc = vop(lambda e, o=G_v[:, nn, :], a=dA_sb[:, ap2, :],
                               b_=q_sb[:, ap2, :],
                               i0=state_sb[:, db * NS + n:db * NS + n + 1]:
                               e.tensor_tensor_scan(o, a, b_, i0, ALU.mult, ALU.add))
                    dAq_war[ap2] = t_sc
                    vdrain()
                    vop(lambda e, o=state_sb[:, db * NS + n:db * NS + n + 1],
                        i=G_v[:, nn, TC - 1:TC]: e.tensor_copy(o, i))
                    t_cm = vop(lambda e, o=G_v[:, nn, :], a=G_v[:, nn, :],
                               b_=cbc_sb[:, rot, :]: e.tensor_tensor(o, a, b_, ALU.mult))
                    cbc_war[rot] = t_cm
                    prev_scan_v = t_cm
                if DEBUG and c == 0 and db == 0 and nq == 3:
                    if not hasattr(build_nc, "_dbgG"):
                        build_nc._dbgG = nc.dram_tensor("dbg_G", [128, 4, TC], BF16,
                                                        kind="ExternalOutput").ap()
                        build_nc._dbgQ = nc.dram_tensor("dbg_q", [128, 2, TC], FP16,
                                                        kind="ExternalOutput").ap()
                        build_nc._dbgA = nc.dram_tensor("dbg_dA", [128, 2, TC], F32,
                                                        kind="ExternalOutput").ap()
                        build_nc._dbgB = nc.dram_tensor("dbg_bb", [128, 2, TC], FP16,
                                                        kind="ExternalOutput").ap()
                        build_nc._dbgC2 = nc.dram_tensor("dbg_cc2", [128, 2, TC], FP16,
                                                         kind="ExternalOutput").ap()
                    S.wait("gpsimd", prev_scan_v)
                    dma("gpsimd", build_nc._dbgG[:, :, :], G_v[:, :, :], sem="dbg")
                    dma("gpsimd", build_nc._dbgQ[:, :, :], q_sb[:, :, :], sem="dbg")
                    dma("gpsimd", build_nc._dbgA[:, :, :], dA_sb[:, :, :], sem="dbg")
                    dma("gpsimd", build_nc._dbgB[:, :, :], bbc_sb[:, :, :], sem="dbg")
                    dma("gpsimd", build_nc._dbgC2[:, :, :], cbc_sb[:, :, :], sem="dbg")
                    dtok = S.mark("dbg")
                    S.wait("vector", dtok)
                vop(lambda e: e.tensor_tensor(L1_v[:, 0:2, :], G_v[:, 0:2, :],
                                              G_v[:, 2:4, :], ALU.add))
                vop(lambda e: e.tensor_tensor(l2f_v, L1_v[:, 0, :],
                                              L1_v[:, 1, :], ALU.add))
                t_ya = vop(lambda e, o=y_sb[:, db, :]: e.tensor_tensor(
                    o, o, l2f_v, ALU.add))
                prev_scan_v = t_ya
            if DEBUG and c == 0 and db == 0:
                if not hasattr(build_nc, "_dbgY0"):
                    build_nc._dbgY0 = nc.dram_tensor("dbg_y0", [128, TC], F32,
                                                     kind="ExternalOutput").ap()
                S.wait("gpsimd", prev_scan_v)
                dma("gpsimd", build_nc._dbgY0[:, :], y_sb[:, 0, :], sem="dbg")
                dtok0 = S.mark("dbg")
                S.wait("vector", dtok0)
        y_all_v = S.mark("vv")
        if DEBUG and c == 0:
            dbg_y = nc.dram_tensor("dbg_y", [128, NDB, TC], F32, kind="ExternalOutput").ap()
            dbg_dl = nc.dram_tensor("dbg_dl", [128, NDB, TC], F32, kind="ExternalOutput").ap()
            dbg_du = nc.dram_tensor("dbg_du", [128, NDB, TC], FP16, kind="ExternalOutput").ap()
            S.wait("gpsimd", y_all_v)
            dma("gpsimd", dbg_y[:, :, :], y_sb[:, :, :], sem="dbg")
            dma("gpsimd", dbg_dl[:, :, :], delta_v[:, :, :], sem="dbg")
            dma("gpsimd", dbg_du[:, :, :], du_sb[:, :, :], sem="dbg")
        st_mm = [None] * 4
        S.wait("vector", t_hd)
        for db in range(NDB):
            S.wait("vector", st_mm[0], st_mm[1])
            t_yb = vop(lambda e, o=ybf_v, i=y_sb[:, db, :]: e.tensor_copy(o, i))
            S.wait("scalar", y_all_v, st_mm[2], st_mm[3])
            t_ys = act(ysq_v, y_sb[:, db, :], ACT.Square)
            S.wait("tensor", t_yb, t_ys)
            if db == 0:
                S.wait("tensor", xq_evac[0], xq_evac[1], st_evac[0], st_evac[1],
                       st_evac[2], st_evac[3])
            for th in range(2):
                st_mm[th] = mm(ones_sb, ybf_v[:, th * 512:(th + 1) * 512],
                               pb[4 + th][0:1, :], start=(db == 0), stop=(db == NDB - 1))
                st_mm[2 + th] = mm(ones_sb, ysq_v[:, th * 512:(th + 1) * 512],
                                   pb[6 + th][0:1, :], start=(db == 0), stop=(db == NDB - 1))
        ev_toks = []
        for th in range(2):
            S.wait("vector", st_mm[th], st_mm[2 + th], mrs_war)
            e1 = vop(lambda e, o=r_sy[:, th * 512:(th + 1) * 512], i=pb[4 + th][0:1, :]:
                     e.tensor_copy(o, i))
            e2 = vop(lambda e, o=r_sy2[:, th * 512:(th + 1) * 512], i=pb[6 + th][0:1, :]:
                     e.tensor_copy(o, i))
            st_evac[th] = e1
            st_evac[2 + th] = e2
            ev_toks += [e1, e2]
        S.wait("gpsimd", *ev_toks)
        t_s1 = dma("gpsimd", st_in_d[c, 0:1, :], r_sy, sem="stw")
        t_s2 = dma("gpsimd", st_in_d[c, 1:2, :], r_sy2, sem="stw")
        S.wait("gpsimd", t_s1, t_s2)
        t_star = coll(st_in_d[c], st_out_d[c])
        S.wait("sync", t_star)
        t_r1 = dma("sync", r_sy, st_out_d[c, 0:1, :], sem="sur")
        t_r2 = dma("sync", r_sy2, st_out_d[c, 1:2, :], sem="sur")
        S.wait("vector", t_r1, t_r2)
        vop(lambda e: e.tensor_scalar(r_sy, r_sy, 1.0 / DI, None, ALU.mult))
        vdrain()
        vop(lambda e: e.tensor_scalar(r_sy2, r_sy2, 1.0 / DI, None, ALU.mult))
        vdrain()
        vmu = S.mark("vv")
        S.wait("scalar", vmu)
        t_msq = act(r_tmp, r_sy, ACT.Square)
        S.wait("vector", t_msq)
        vop(lambda e: e.tensor_tensor(r_sy2, r_sy2, r_tmp, ALU.subtract))
        vdrain()
        vop(lambda e: e.tensor_scalar(r_sy2, r_sy2, LN_EPS, None, ALU.add))
        vdrain()
        vdrain()
        v_rc = vop(lambda e: e.reciprocal(r_tmp, r_sy2))
        vdrain()
        S.wait("scalar", v_rc)
        s_sq = act(r_tmp, r_tmp, ACT.Sqrt)
        S.wait("vector", s_sq)
        r_nt = dtmp_sb[0:1, :]
        vrt = S.mark("vv")
        S.wait("scalar", vrt, s_sq)
        t_nsq = act(r_nt, r_tmp, ACT.Square)
        S.wait("vector", t_nsq)
        vop(lambda e: e.tensor_tensor(r_nt, r_nt, r_sy2, ALU.mult))
        vdrain()
        vop(lambda e: e.tensor_scalar(r_nt, r_nt, -0.5, 1.5, ALU.mult, ALU.add))
        vdrain()
        vop(lambda e: e.tensor_tensor(r_sy2, r_tmp, r_nt, ALU.mult))
        vdrain()
        t_m1 = vop(lambda e: e.tensor_copy(r_h16, r_sy))
        S.wait("gpsimd", t_m1, mrs_read_gate[0])
        t_md1 = dma("gpsimd", mrs_d[0:1, :], r_h16, sem="mw1")
        S.wait("vector", t_md1)
        t_m2 = vop(lambda e: e.tensor_copy(r_h16, r_sy2))
        S.wait("gpsimd", t_m2)
        t_md2 = dma("gpsimd", mrs_d[1:2, :], r_h16, sem="mw2")
        mrs_war = t_md2
        S.wait("sync", t_md1, norm_war)
        t_mu = dma("sync", mubc_sb, mrs_d[0:1, :].partition_broadcast(128), sem="mub")
        S.wait("sync", t_md2)
        t_rs = dma("sync", rsbc_sb, mrs_d[1:2, :].partition_broadcast(128), sem="rsb")
        mrs_read_gate[0] = t_rs
        for db in range(NDB):
            S.wait("vector", t_mu, t_rs)
            vop(lambda e, o=y_sb[:, db, :]: e.tensor_tensor(o, o, mubc_sb, ALU.subtract))
            vop(lambda e, o=y_sb[:, db, :]: e.tensor_tensor(o, o, rsbc_sb, ALU.mult))
            vop(lambda e, o=y_sb[:, db, :], g_=cols_sb[:, 4, db:db + 1],
                b_=cols_sb[:, 5, db:db + 1]: e.tensor_scalar(o, o, g_, b_,
                                                             ALU.mult, ALU.add))
            t_gt = vop(lambda e, o=gated_sb[:, db, :], a=y_sb[:, db, :],
                       b_=zsil_sb[:, par, db, :]: e.tensor_tensor(o, a, b_, ALU.mult))
            zsil_war[par] = t_gt
        gate_v = S.mark("vv")
        norm_war = gate_v
        sp_dmas = []
        for mb in range(8):
            for th in range(2):
                bk = 6 + (mb * 2 + th) % 2
                bank = pb[bk]
                S.wait("tensor", gate_v)
                S.wait("tensor", op_evac[(mb * 2 + th) % 2], st_evac[2],
                       st_evac[3])
                toks = [mm(wout_sb[:, db, mb * 128:(mb + 1) * 128],
                           gated_sb[:, db, th * 512:(th + 1) * 512],
                           bank, start=(db == 0), stop=(db == NDB - 1))
                        for db in range(NDB)]
                sv = (mb * 2 + th) % 2
                S.wait("vector", toks[-1], sev_war[sv])
                ev = vop(lambda e, o=sev_sb[:, sv, :], i=bank: e.tensor_copy(o, i))
                op_evac[(mb * 2 + th) % 2] = ev
                S.wait("gpsimd", ev)
                t_sp = dma("gpsimd",
                           sp_in_d[c // 2, mb * 128:(mb + 1) * 128,
                                   (c % 2) * TC + th * 512:(c % 2) * TC + (th + 1) * 512],
                           sev_sb[:, sv, :], sem="sp0" if sv == 0 else "sp1")
                sp_dmas.append(t_sp)
                sev_war[sv] = t_sp
        if c == NCH - 1:
            S.wait("gpsimd", *sp_dmas)
            s_rs_tok = coll(sp_in_d, s_own_d, op=ALU.add, kind="ReduceScatter")

    S.wait("sync", s_rs_tok, p1_g)
    S.wait("sync", S.mark("vv"))
    out_dmas = []
    x3_war = [None, None]
    for tt in range(TQ // 128):
        rp = tt % 2
        S.wait("sync", x3_war[rp])
        t_x = dma("sync", xt_sb[:, rp, :], x_own[tt * 128:(tt + 1) * 128, :], sem="x30" if rp == 0 else "x31")
        t_s = tdma("sync", s3_v[:, rp, :], s_own_d[:, tt * 128:(tt + 1) * 128], sem="s30" if rp == 0 else "s31")
        t_g = tdma("sync", g3_v[:, rp, :], g_own_d[:, tt * 128:(tt + 1) * 128], sem="g30" if rp == 0 else "g31")
        S.wait("vector", t_x, t_s, t_g)
        vop(lambda e, o=o3_v[:, rp, :], a=s3_v[:, rp, :], b_=g3_v[:, rp, :]:
            e.tensor_tensor(o, a, b_, ALU.mult))
        t_o = vop(lambda e, o=o3_v[:, rp, :], a=xt_sb[:, rp, :]:
                  e.tensor_tensor(o, o, a, ALU.add))
        x3_war[rp] = t_o
        S.wait("gpsimd", t_o)
        t_od = dma("gpsimd", out_d[tt * 128:(tt + 1) * 128, :], o3_v[:, rp, :], sem="ow")
        out_dmas.append(t_od)
    S.wait("gpsimd", *out_dmas)
    if DEBUG:
        dbg_xn = nc.dram_tensor("dbg_xn", [DM, TQ], F32, kind="ExternalOutput").ap()
        dbg_g = nc.dram_tensor("dbg_g", [DM, TQ], FP16, kind="ExternalOutput").ap()
        dbg_ssm = nc.dram_tensor("dbg_ssm", [NCH, 33, TC], F32, kind="ExternalOutput").ap()
        dbg_s = nc.dram_tensor("dbg_s", [DM, TQ], FP16, kind="ExternalOutput").ap()
        dbg_c = nc.dram_tensor("dbg_c", [128, 16, 4], F32, kind="ExternalOutput").ap()
        dbg_na = nc.dram_tensor("dbg_na", [128, 64], F32, kind="ExternalOutput").ap()
        dma("gpsimd", dbg_na[:, :], negA_sb[:, :], sem="dbg")
        dbg_xa = nc.dram_tensor("dbg_xa", [4 * DM, TQ], F32, kind="ExternalOutput").ap()
        dma("gpsimd", dbg_xa[:, :], xn_all_d[:, :], sem="dbg")
        S.wait("gpsimd", S.mark("vv"))
        dma("gpsimd", dbg_c[:, :, :], dbgc_sb[:, :, :], sem="dbg")
        dtoks = [dma("gpsimd", dbg_xn[:, :], xn_own_d[:, :], sem="dbg"),
                 dma("gpsimd", dbg_g[:, :], g_own_d[:, :], sem="dbg"),
                 dma("gpsimd", dbg_ssm[:, :, :], ssm_out_d[:, :, :], sem="dbg"),
                 dma("gpsimd", dbg_s[:, :], s_own_d[:, :], sem="dbg")]
        S.wait("gpsimd", *dtoks)

    with nc.Block() as block:
        @block.sync
        def _(e):
            e._sems = sems
            for fn in S.ops["sync"]:
                fn(e)

        @block.gpsimd
        def _(e):
            e._sems = sems
            for fn in S.ops["gpsimd"]:
                fn(e)

        @block.scalar
        def _(e):
            e._sems = sems
            for fn in S.ops["scalar"]:
                fn(e)

        @block.vector
        def _(e):
            e._sems = sems
            for fn in S.ops["vector"]:
                fn(e)

        @block.tensor
        def _(e):
            e._sems = sems
            for fn in S.ops["tensor"]:
                fn(e)

    stk.close()
    return nc


def _host_shards(inputs):
    import jax
    x = np.ascontiguousarray(np.asarray(inputs["L0_attn_x"], dtype=np.float32))
    Win = np.asarray(inputs["L0_attn_in_proj_w"], dtype=np.float32)
    convw = np.asarray(inputs["L0_attn_conv_w"], dtype=np.float32)
    convb = np.asarray(inputs["L0_attn_conv_b"], dtype=np.float32)
    xpw = np.asarray(inputs["L0_attn_x_proj_w"], dtype=np.float32)
    dtw = np.asarray(inputs["L0_attn_dt_proj_w"], dtype=np.float32)
    dtb = np.asarray(inputs["L0_attn_dt_proj_b"], dtype=np.float32)
    logA = np.asarray(inputs["L0_attn_log_A"], dtype=np.float32)
    Dp = np.asarray(inputs["L0_attn_D_param"], dtype=np.float32)
    nw = np.asarray(inputs["L0_attn_norm_w"], dtype=np.float32)
    nb = np.asarray(inputs["L0_attn_norm_b"], dtype=np.float32)
    Wout = np.asarray(inputs["L0_attn_out_proj_w"], dtype=np.float32)
    W1 = np.asarray(inputs["L0_ffn_W1"], dtype=np.float32)
    b1 = np.asarray(inputs["L0_ffn_b1"], dtype=np.float32)
    W2 = np.asarray(inputs["L0_ffn_W2"], dtype=np.float32)
    b2 = np.asarray(inputs["L0_ffn_b2"], dtype=np.float32)

    with jax.default_device(jax.devices("cpu")[0]):
        keep = np.asarray(jax.random.bernoulli(
            jax.random.key(42, impl="threefry2x32"), 0.8, (B, T, DM)))
    maskv = np.where(keep, np.float16(1.25), np.float16(0.0)).astype(np.float16)

    ident = np.eye(128, dtype=np.float32)
    maps = []
    for core in range(8):
        b_, qd = core // 4, core % 4
        dsl = slice(qd * DSL, (qd + 1) * DSL)
        tsl = slice(qd * TQ, (qd + 1) * TQ)
        W_sl = np.concatenate([Win[dsl], Win[DI + qd * DSL: DI + (qd + 1) * DSL]], axis=0)
        cvd = np.zeros((NDB * 4, 128, 128), np.float32)
        for db in range(NDB):
            for tap in range(4):
                np.fill_diagonal(cvd[db * 4 + tap],
                                 convw[qd * DSL + db * 128: qd * DSL + (db + 1) * 128, 0, tap])

        def colify(v, ncol):
            return np.ascontiguousarray(v.reshape(ncol, 128).T)

        m = {
            "x_own": np.ascontiguousarray(x[b_, tsl]),
            "winT": np.ascontiguousarray(W_sl.T),
            "convd": cvd,
            "convb": colify(convb[dsl], NDB),
            "xpwT": np.ascontiguousarray(xpw[:, dsl].T),
            "dtw": colify(dtw[dsl, 0], NDB),
            "dtb": colify(dtb[dsl], NDB),
            "logA": np.ascontiguousarray(logA[dsl].reshape(NDB, 128, NS)),
            "Dprm": colify(Dp[dsl], NDB),
            "nw": colify(nw[dsl], NDB),
            "nb": colify(nb[dsl], NDB),
            "woutT": np.ascontiguousarray(Wout[:, dsl].T),
            "w1T": np.ascontiguousarray(W1.T),
            "b1": colify(b1, 32),
            "w2T": np.ascontiguousarray(W2.T),
            "b2": colify(b2, 8),
            "maskT": np.ascontiguousarray(maskv[b_, tsl].T),
            "ident": ident,
            "qsel": np.asarray([[1 if p == qd else 0 for p in range(4)]], np.int32),
        }
        maps.append(m)
    return maps


def kernel(**inputs):
    maps = _host_shards(inputs)
    nc = build_nc()
    res = run_bass_kernel_spmd(nc, maps, core_ids=list(range(8)))
    out = np.empty((B, T, DM), np.float32)
    for core in range(8):
        b_, qd = core // 4, core % 4
        out[b_, qd * TQ:(qd + 1) * TQ] = res.results[core]["out"]
    return out
